# version 06
# baseline (speedup 1.0000x reference)
import math
from contextlib import ExitStack

import numpy as np
import ml_dtypes

import concourse.bass as bass
import concourse.mybir as mybir
from concourse.bass_core_rust import IndirectOffsetOnAxis
from concourse.bass_utils import run_bass_kernel_spmd

F32 = mybir.dt.float32
BF16 = mybir.dt.bfloat16
I32 = mybir.dt.int32
AF = mybir.ActivationFunctionType
ALU = mybir.AluOpType
AX = mybir.AxisListType

B, S, DM = 4, 4096, 4096
HKV, G, HD = 8, 4, 128
NH = HKV * G
QR = 2048
NSLOT, SLOT_Q = 8, 256
L_SLOT = [4 * (i + 1) for i in range(NSLOT)]
SLOPES = [2.0 ** (-(h + 1) / 4.0) for h in range(NH)]
W_HEAD = [64] * 4 + [128] * 4 + [256] * 24
KEEP_H = [min(32, 4 + math.ceil(2.0 / SLOPES[h])) for h in range(NH)]
SHIFT = 10.0
ISQ = 1.0 / math.sqrt(HD)
NEG = -1e30
MT = DM // 128
QT = QR // 128
NCORE = 8
EPS_RMS = 1e-6
EPS_LN = 1e-5


def subwidths(h, i):
    if i == 0:
        if h < 4:
            return [16, 16, 16, 16, 32, 32, 64, 64]
        if h < 8:
            return [32, 32, 64, 128]
        if h < 12:
            return [128, 128]
    W = W_HEAD[h]
    return [W] * (SLOT_Q // W)


def c0_of(h, i):
    return max(0, L_SLOT[i] - KEEP_H[h])


def bias_layout():
    off, tot = {}, 0
    for i in range(NSLOT):
        for g in range(HKV):
            off[(i, g)] = tot
            for hl in range(G):
                h = 4 * g + hl
                tot += (L_SLOT[i] - c0_of(h, i)) * len(subwidths(h, i))
    return tot, off


BIAS_TOT, BIAS_OFF = bias_layout()


def bias_col(i, g, hl, c, j):
    h = 4 * g + hl
    idx = 0
    for hl2 in range(hl):
        h2 = 4 * g + hl2
        idx += (L_SLOT[i] - c0_of(h2, i)) * len(subwidths(h2, i))
    idx += (c - c0_of(h, i)) * len(subwidths(h, i)) + j
    return idx


def core_ranges(core):
    half = core % 2
    return [2 * i + 1 - half for i in range(NSLOT)]


def build_tables(core):
    ranges = core_ranges(core)
    bias = np.empty((128, BIAS_TOT), dtype=np.float32)
    p = np.arange(128, dtype=np.float64)
    for i in range(NSLOT):
        for g in range(HKV):
            base = BIAS_OFF[(i, g)]
            for hl in range(G):
                h = 4 * g + hl
                ws = subwidths(h, i)
                q_ends = np.cumsum(ws) + 256 * ranges[i]
                for c in range(c0_of(h, i), L_SLOT[i]):
                    s_abs = 128 * c + p
                    for j, q_end in enumerate(q_ends):
                        col = base + bias_col(i, g, hl, c, j)
                        v = SLOPES[h] * (s_abs - q_end) - SHIFT
                        bias[:, col] = np.where(s_abs >= q_end, NEG, v).astype(np.float32)
    mask = np.zeros((NSLOT, 4, 128, 4 * SLOT_Q), dtype=np.float32)
    for i in range(NSLOT):
        L = L_SLOT[i]
        r = ranges[i]
        for mm in range(4):
            s_abs = 128 * (L - 4 + mm) + np.arange(128)[:, None]
            q_abs = 256 * r + np.arange(SLOT_Q)[None, :]
            mask[i, mm] = np.tile((s_abs <= q_abs).astype(np.float32), (1, 4))
    return bias, mask.astype(ml_dtypes.bfloat16)


ENGS = ("sync", "tensor", "vector", "scalar", "gpsimd")


class Prog:
    def __init__(self):
        self.items = {e: [] for e in ENGS}

    def add(self, eng, fn):
        self.items[eng].append(fn)


class Ctr:
    def __init__(self, handle, name=""):
        self.h, self.n, self.name = handle, 0, name

    def inc(self, k=1):
        self.n += k
        return self.n


class Alloc:
    def __init__(self, tensor, n, name):
        self.t, self.n, self.off, self.name = tensor, n, 0, name
        self.marks, self.hi = [], 0

    def take(self, cols, align=32):
        self.off = (self.off + align - 1) // align * align
        assert self.off + cols <= self.n, f"{self.name} overflow {self.off}+{cols}>{self.n}"
        ap = self.t[:, self.off:self.off + cols]
        self.off += cols
        self.hi = max(self.hi, self.off)
        return ap

    def push(self):
        self.marks.append(self.off)

    def pop(self):
        self.off = self.marks.pop()


def build_program(upto=99, debug=False):
    nc = bass.Bass("TRN2", target_bir_lowering=False)
    dk = "ExternalOutput" if debug else "Internal"

    x_q = nc.dram_tensor("x_q", [QR, DM], F32, kind="ExternalInput").ap()
    x_qT = nc.dram_tensor("x_qT", [DM, QR], F32, kind="ExternalInput").ap()
    x_kvT = nc.dram_tensor("x_kvT", [DM, S], F32, kind="ExternalInput").ap()
    WqT = nc.dram_tensor("WqT", [DM, DM], F32, kind="ExternalInput").ap()
    WkT = nc.dram_tensor("WkT", [DM, 1024], F32, kind="ExternalInput").ap()
    WvT = nc.dram_tensor("WvT", [DM, 1024], F32, kind="ExternalInput").ap()
    WoT = nc.dram_tensor("WoT", [DM, DM], F32, kind="ExternalInput").ap()
    qw = nc.dram_tensor("qw", [1, HD], F32, kind="ExternalInput").ap()
    kw = nc.dram_tensor("kw", [1, HD], F32, kind="ExternalInput").ap()
    upw = nc.dram_tensor("upw", [16, 256, 1024], F32, kind="ExternalInput").ap()
    downw = nc.dram_tensor("downw", [16, 1024, 256], F32, kind="ExternalInput").ap()
    perm = nc.dram_tensor("perm", [DM], I32, kind="ExternalInput").ap()
    bias_tab = nc.dram_tensor("bias_tab", [128, BIAS_TOT], F32, kind="ExternalInput").ap()
    mask_tab = nc.dram_tensor("mask_tab", [NSLOT, 4, 128, 4 * SLOT_Q], BF16, kind="ExternalInput").ap()
    ones_row = nc.dram_tensor("ones_row", [1, 128], F32, kind="ExternalInput").ap()

    out = nc.dram_tensor("out", [QR, DM], F32, kind="ExternalOutput").ap()

    xqT_bf = nc.dram_tensor("xqT_bf", [DM, QR], BF16, kind="Internal").ap()
    xkvT_bf = nc.dram_tensor("xkvT_bf", [DM, S], BF16, kind="Internal").ap()
    wqT_bf = nc.dram_tensor("wqT_bf", [DM, DM], BF16, kind="Internal").ap()
    wkT_bf = nc.dram_tensor("wkT_bf", [DM, 1024], BF16, kind="Internal").ap()
    wvT_bf = nc.dram_tensor("wvT_bf", [DM, 1024], BF16, kind="Internal").ap()
    woT_bf = nc.dram_tensor("woT_bf", [DM, DM], BF16, kind="Internal").ap()
    khat = nc.dram_tensor("khat", [S, 1024], BF16, kind=dk).ap()
    vhat = nc.dram_tensor("vhat", [S, 1024], BF16, kind=("ExternalOutput" if debug else "Internal")).ap()
    qhat = nc.dram_tensor("qhat", [QR, DM], BF16, kind=dk).ap()
    outT = nc.dram_tensor("outT", [DM, QR], BF16, kind=dk).ap()
    v_scr = nc.dram_tensor("v_scr", [QR, DM], F32, kind=dk).ap()
    y1b = nc.dram_tensor("y1b", [QR, DM], BF16, kind=dk).ap()
    y1T = nc.dram_tensor("y1T", [DM, QR], BF16, kind="Internal").ap()
    y2T = nc.dram_tensor("y2T", [DM, QR], BF16, kind=dk).ap()

    prog = Prog()

    with ExitStack() as es:
        ec = es.enter_context
        BFN, F3N, SMN = 88 * 1024, 4608, 1536
        ab = Alloc(ec(nc.sbuf_tensor("arena_bf", [128, BFN], BF16)), BFN, "bf")
        af = Alloc(ec(nc.sbuf_tensor("arena_f32", [128, F3N], F32)), F3N, "f32")
        asm = Alloc(ec(nc.sbuf_tensor("arena_small", [128, SMN], F32)), SMN, "small")
        ai = Alloc(ec(nc.sbuf_tensor("arena_i32", [128, 64], I32)), 64, "i32")
        row_f = ec(nc.sbuf_tensor("row_f", [1, 1536], F32))

        ps = [ec(nc.psum_tensor(f"ps{i}", [128, 512], F32)) for i in range(8)]

        names = ("ldA", "ldB", "ldC", "ldD", "te", "ve", "se", "gp", "gsc",
                 "stA", "stB", "stC", "s_sc", "s_px", "s_pm", "s_av", "s_fin")
        sems = {n: Ctr(ec(nc.semaphore(n)), n) for n in names}
        ldA, ldB, ldC = sems["ldA"], sems["ldB"], sems["ldC"]
        te, ve, se, gp = sems["te"], sems["ve"], sems["se"], sems["gp"]
        stA, stB = sems["stA"], sems["stB"]
        s_sc, s_px, s_pm = sems["s_sc"], sems["s_px"], sems["s_pm"]
        s_av, s_fin = sems["s_av"], sems["s_fin"]
        gsc = sems["gsc"]
        ldD, stC = sems["ldD"], sems["stC"]

        block = ec(nc.Block())

        def w(eng, sem, val):
            if val > 0:
                prog.add(eng, lambda e, s=sem.h, v=val: e.wait_ge(s, v))

        def dma(eng, sem, dst, src, **kw):
            v = sem.inc(16)
            prog.add(eng, lambda e, d=dst, s=src, k=dict(kw):
                     e.dma_start(out=d, in_=s, **k).then_inc(sem.h, 16))
            return v

        def dma_t(eng, sem, dst, src):
            v = sem.inc(16)
            prog.add(eng, lambda e, d=dst, s=src:
                     e.dma_start(out=d, in_=s, transpose=True).then_inc(sem.h, 16))
            return v

        def barrier():
            vals = [(c.h, c.n) for c in sems.values() if c.n > 0]
            for e in ENGS:
                for hh, v in vals:
                    prog.add(e, lambda eng, s=hh, vv=v: eng.wait_ge(s, vv))

        def t_op(fn, inc=None):
            if inc is None:
                prog.add("tensor", fn)
                return None
            v = inc.inc(1)
            prog.add("tensor", lambda e, f=fn: f(e).then_inc(inc.h, 1))
            return v

        def v_op(fn, inc=None):
            c = ve if inc is None else inc
            v = c.inc(1)
            prog.add("vector", lambda e, f=fn, cc=c: f(e).then_inc(cc.h, 1))
            return v

        def s_op(fn, inc=None):
            c = se if inc is None else inc
            v = c.inc(1)
            prog.add("scalar", lambda e, f=fn, cc=c: f(e).then_inc(cc.h, 1))
            return v

        def vsync():
            w("vector", ve, ve.n)

        def ssync():
            w("scalar", se, se.n)

        ones_b = ab.take(32)[:, 0:1]
        wq_t = af.take(128)
        wk_t = af.take(128)
        onesrow = row_f[0:1, 0:128]
        qw_sb = row_f[0:1, 128:256]
        kw_sb = row_f[0:1, 256:384]
        perm_sb = ai.take(32)

        prog.add("vector", lambda e: e.memset(ones_b, 1.0))
        v_perm = dma("sync", ldA, perm_sb, perm.rearrange("(t p) -> p t", p=128), allow_slow_non_contiguous=True)
        dma("sync", ldA, onesrow, ones_row)
        dma("sync", ldA, qw_sb, qw)
        v_rows = dma("sync", ldA, kw_sb, kw)
        v_ldA_ph0 = ldA.n
        w("tensor", ldA, ldA.n)
        t_op(lambda e: e.matmul(ps[7][:, 0:128], onesrow, qw_sb, start=True, stop=True))
        vt = t_op(lambda e: e.matmul(ps[7][:, 128:256], onesrow, kw_sb, start=True, stop=True), te)
        w("vector", te, vt)
        v_op(lambda e: e.tensor_copy(wq_t, ps[7][:, 0:128]))
        v_op(lambda e: e.tensor_copy(wk_t, ps[7][:, 128:256]), ve)

        def cast(dst, src, npiece):
            rows = src.shape[0] // npiece
            vlast = 0
            for pp in range(npiece):
                vlast = dma("gpsimd", gp, dst[pp * rows:(pp + 1) * rows, :],
                            src[pp * rows:(pp + 1) * rows, :])
            return vlast

        def cast_group(dst, src, npiece):
            v = cast(dst, src, npiece)
            w("gpsimd", gp, v)
            return v

        v_xq = cast_group(xqT_bf, x_qT, 4)
        v_wq_a = cast_group(wqT_bf[:, 0:2048], WqT[:, 0:2048], 4)
        v_wq_b = cast_group(wqT_bf[:, 2048:4096], WqT[:, 2048:4096], 4)
        v_wk = cast_group(wkT_bf, WkT, 2)
        v_xkv = cast_group(xkvT_bf, x_kvT, 8)
        v_wv = cast_group(wvT_bf, WvT, 2)
        v_wo = cast_group(woT_bf, WoT, 8)


        def rms_evac(pA, pB, sq, cols, wnorm_t, ot, gt):
            w("scalar", te, te.n)
            if gt >= 2:
                w("vector", stA, stA.n)
            s_op(lambda e, a=pA: e.activation(sq[:, 0:512], a[:, 0:512], AF.Square))
            vsq = s_op(lambda e, b=pB: e.activation(sq[:, 512:1024], b[:, 0:512], AF.Square), se)
            w("vector", se, vsq)
            for hh in range(8):
                v_op(lambda e, h=hh: e.reduce_sum(
                    cols[:, h:h + 1], sq[:, h * 128:(h + 1) * 128], axis=AX.X))
            vsync()
            v_op(lambda e: e.tensor_scalar(cols[:, 8:16], cols[:, 0:8],
                                           1.0 / HD, EPS_RMS, ALU.mult, ALU.add))
            vsync()
            vrec = v_op(lambda e: e.reciprocal(cols[:, 16:24], cols[:, 8:16]), ve)
            w("scalar", ve, vrec)
            vsq2 = s_op(lambda e: e.activation(cols[:, 24:32], cols[:, 16:24], AF.Sqrt), se)
            w("vector", se, vsq2)
            for hh in range(8):
                pp_ = pA if hh < 4 else pB
                off = (hh % 4) * 128
                fin = ve if hh == 7 else None
                v_op(lambda e, p=pp_, o=off, h=hh, k=ot: e.scalar_tensor_tensor(
                    k[:, h * 128:(h + 1) * 128], p[:, o:o + 128],
                    cols[:, 24 + h:25 + h], wnorm_t,
                    ALU.mult, ALU.mult), fin)
            w("sync", ve, ve.n)

        if upto >= 2:
            ab.push(); af.push()
            wqT_h = ab.take(MT * 2048)
            xq_stage = [ab.take(MT * 128) for _ in range(2)]
            qtile = [ab.take(1024) for _ in range(2)]
            sq_q = af.take(1024)
            cols_q = asm.take(64)

            def stage_qt(qt, gate):
                if gate:
                    w("sync", te, te.n)
                w("sync", ldC, ldC.n)
                dma("sync", ldC, xq_stage[qt % 2].rearrange("p (a q) -> p a q", q=128),
                    xqT_bf[:, qt * 128:(qt + 1) * 128].rearrange("(a p) q -> p a q", p=128))

            w("sync", gp, v_xq)
            it = 0
            for dh in range(2):
                w("sync", gp, v_wq_a if dh == 0 else v_wq_b)
                w("sync", te, te.n)
                w("sync", ldB, ldB.n)
                dma("sync", ldB, wqT_h.rearrange("p (a d) -> p a d", d=2048),
                    wqT_bf[:, dh * 2048:(dh + 1) * 2048].rearrange("(a p) d -> p a d", p=128))
                v_wqT = ldB.n
                for qt in range(QT):
                    if dh == 0 and qt == 0:
                        stage_qt(0, False)
                    if qt + 1 < QT:
                        stage_qt(qt + 1, True)
                    elif dh == 0:
                        stage_qt(0, True)
                    stg = xq_stage[qt % 2]
                    w("tensor", ldC, ldC.n)
                    if qt == 0:
                        w("tensor", ldB, v_wqT)
                    for dq2 in range(2):
                        dq = 2 * dh + dq2
                        bk = it % 2
                        pA, pB = ps[2 * bk], ps[2 * bk + 1]
                        if it >= 2:
                            w("tensor", ve, ve.n)
                        for mc in range(MT):
                            lt = stg[:, mc * 128:(mc + 1) * 128]
                            first = (mc == 0)
                            t_op(lambda e, l=lt, m=mc, f=first, a=pA, d2=dq2: e.matmul(
                                a[:, 0:512], l,
                                wqT_h[:, m * 2048 + d2 * 1024: m * 2048 + d2 * 1024 + 512],
                                start=f, stop=(m == MT - 1)))
                            fin = te if mc == MT - 1 else None
                            t_op(lambda e, l=lt, m=mc, f=first, b=pB, d2=dq2: e.matmul(
                                b[:, 0:512], l,
                                wqT_h[:, m * 2048 + d2 * 1024 + 512: m * 2048 + (d2 + 1) * 1024],
                                start=f, stop=(m == MT - 1)), fin)
                        ot = qtile[it % 2]
                        rms_evac(pA, pB, sq_q, cols_q, wq_t, ot, it)
                        dma("sync", stA, qhat[qt * 128:(qt + 1) * 128,
                                              dq * 1024:(dq + 1) * 1024], ot)
                        it += 1
            ab.pop(); af.pop()
            barrier()

        def proj_rms(n_tiles, emit_stage, cur_stage, WT, wnorm_t, dst_fn, sq, cols, otiles, t_base=0):
            for t in range(n_tiles):
                emit_stage(t)
                gt = t_base + t
                bk = gt % 2
                pA, pB = ps[2 * bk], ps[2 * bk + 1]
                if gt >= 2:
                    w("tensor", ve, ve.n)
                for mc in range(MT):
                    lt = cur_stage(t, mc)
                    first = (mc == 0)
                    t_op(lambda e, l=lt, m=mc, f=first, a=pA: e.matmul(
                        a[:, 0:512], l, WT[:, m * 1024:m * 1024 + 512],
                        start=f, stop=(m == MT - 1)))
                    fin = te if mc == MT - 1 else None
                    t_op(lambda e, l=lt, m=mc, f=first, b=pB: e.matmul(
                        b[:, 0:512], l, WT[:, m * 1024 + 512:(m + 1) * 1024],
                        start=f, stop=(m == MT - 1)), fin)
                ot = otiles[gt % 2]
                rms_evac(pA, pB, sq, cols, wnorm_t, ot, gt)
                dst_fn(t, ot)

        if upto >= 2:
            ab.push()
            V_sb = ab.take(MT * 1024)
            ab.push(); af.push()
            WT_sb = ab.take(MT * 1024)
            pan_sb = ab.take(MT * 512)
            ktile = [ab.take(1024) for _ in range(2)]
            sq_k = af.take(1024)
            cols_k = asm.take(64)

            w("sync", gp, v_wk)
            dma("sync", ldB, WT_sb.rearrange("p (a d) -> p a d", d=1024),
                wkT_bf.rearrange("(a p) d -> p a d", p=128))
            v_wkT = ldB.n
            w("sync", gp, v_xkv)

            pan3 = pan_sb.rearrange("p (a s) -> p a s", s=512)

            def stage_half(H):
                if H >= 2:
                    w("sync", te, te.n)
                hp = H % 2
                dma("sync", ldB, pan3[:, :, hp * 256:(hp + 1) * 256],
                    xkvT_bf.rearrange("(a p) s -> p a s", p=128)[:, :, H * 256:(H + 1) * 256])

            def cur_stage_k(t, mc):
                st = t % 4
                return pan_sb[:, mc * 512 + st * 128: mc * 512 + (st + 1) * 128]

            def emit_stage_k(t):
                if t == 0:
                    stage_half(0)
                    stage_half(1)
                elif t % 2 == 0 and t // 2 + 1 < 16:
                    stage_half(t // 2 + 1)
                w("tensor", ldB, ldB.n)

            def dst_k(t, ot):
                dma("sync", stA, khat[t * 128:(t + 1) * 128, :], ot)

            proj_rms(32, emit_stage_k, cur_stage_k, WT_sb, wk_t, dst_k, sq_k, cols_k, ktile)

            w("sync", gp, v_wv)
            w("sync", te, te.n)
            w("sync", ldB, ldB.n)
            dma("sync", ldB, WT_sb.rearrange("p (a d) -> p a d", d=1024),
                wvT_bf.rearrange("(a p) d -> p a d", p=128))
            v_wvT = ldB.n

            for t in range(32):
                if t == 0:
                    stage_half(0)
                    stage_half(1)
                elif t % 2 == 0 and t // 2 + 1 < 16:
                    stage_half(t // 2 + 1)
                w("tensor", ldB, ldB.n)
                bk = t % 2
                pA, pB = ps[2 * bk], ps[2 * bk + 1]
                w("tensor", ve, ve.n)
                for mc in range(MT):
                    lt = cur_stage_k(t, mc)
                    first = (mc == 0)
                    t_op(lambda e, l=lt, m=mc, f=first, a=pA: e.matmul(
                        a[:, 0:512], l, WT_sb[:, m * 1024:m * 1024 + 512],
                        start=f, stop=(m == MT - 1)))
                    fin = te if mc == MT - 1 else None
                    t_op(lambda e, l=lt, m=mc, f=first, b=pB: e.matmul(
                        b[:, 0:512], l, WT_sb[:, m * 1024 + 512:(m + 1) * 1024],
                        start=f, stop=(m == MT - 1)), fin)
                v_mm = te.n
                w("vector", te, v_mm)
                v_op(lambda e, sg=t, a=pA: e.tensor_copy(V_sb[:, sg * 1024: sg * 1024 + 512],
                                                         a[:, 0:512]))
                v_op(lambda e, sg=t, b=pB: e.tensor_copy(V_sb[:, sg * 1024 + 512:(sg + 1) * 1024],
                                                         b[:, 0:512]), ve)
                if debug:
                    w("sync", ve, ve.n)
                    dma("sync", stA, vhat[t * 128:(t + 1) * 128, :],
                        V_sb[:, t * 1024:(t + 1) * 1024])
            ab.pop(); af.pop()
            barrier()

        if upto >= 4:
            ab.push(); af.push()
            KT_sb = ab.take(HKV * S)
            qT_sl = [[ab.take(256) for _ in range(G)] for _ in range(2)]
            phat = [ab.take(4 * 256) for _ in range(2)]
            mask_st = ab.take(4 * 1024)
            outt_t = [ab.take(256) for _ in range(4)]
            bias_st = af.take(1024)
            bc_sb = af.take(512)
            rs_sb = row_f[0:1, 512:1024]

            w("sync", stA, stA.n)
            for kh in range(HKV):
                dma_t("sync", ldB, KT_sb[:, kh * S:(kh + 1) * S],
                      khat[:, kh * 128:(kh + 1) * 128])
            v_kt = ldB.n
            w("tensor", ldB, v_kt)

            n_chunk, n_mask, n_fin, n_qld = 0, 0, 0, 0
            for i in range(NSLOT):
                L = L_SLOT[i]
                w("sync", ve, ve.n)
                dma("sync", ldA, mask_st.rearrange("p (c q) -> p c q", c=4),
                    mask_tab[i].rearrange("c p q -> p c q"))
                v_mst = ldA.n
                for g in range(HKV):
                    ncols = sum((L - c0_of(4 * g + hl, i)) * len(subwidths(4 * g + hl, i))
                                for hl in range(G))
                    w("scalar", ldC, ldC.n)
                    w("scalar", se, se.n)
                    w("scalar", s_px, s_px.n)
                    v_bst = dma("scalar", ldC, bias_st[:, 0:ncols],
                                bias_tab[:, BIAS_OFF[(i, g)]:BIAS_OFF[(i, g)] + ncols])
                    sl = qT_sl[n_qld % 2]
                    if n_qld >= 2:
                        w("sync", te, te.n)
                    w("sync", ldB, ldB.n)
                    for hl in range(G):
                        dma_t("sync", ldB, sl[hl],
                              qhat[i * 256:(i + 1) * 256,
                                   (4 * g + hl) * 128:(4 * g + hl + 1) * 128])
                    v_qsl = ldB.n
                    n_qld += 1

                    cmin = min(c0_of(4 * g + hl, i) for hl in range(G))
                    started_av = {0: False, 1: False}
                    started_sum = {0: False, 1: False}
                    v_fin_prev = s_fin.n
                    first_av_emitted = [False]
                    w("tensor", ldB, v_qsl)
                    w("scalar", ldC, v_bst)
                    if g == 0:
                        w("vector", ldA, v_mst)

                    def emit_av(pend):
                        c, pp2, acts2, mskd, v_exp2, v_pm2 = pend
                        if not first_av_emitted[0]:
                            w("tensor", s_fin, v_fin_prev)
                            first_av_emitted[0] = True
                        if mskd:
                            w("tensor", s_pm, v_pm2)
                        else:
                            w("tensor", s_px, v_exp2)
                        for hl in acts2:
                            bnk = hl // 2
                            acts_b = [a for a in acts2 if a // 2 == bnk]
                            stv = (c == L - 1 and hl == acts_b[-1])
                            t_op(lambda e, h=hl, cc=c, p=pp2, f=(not started_av[bnk]),
                                 g2=g, sv=stv: e.matmul(
                                ps[4 + h // 2][:, (h % 2) * 256:(h % 2 + 1) * 256],
                                V_sb[:, cc * 1024 + g2 * 128: cc * 1024 + (g2 + 1) * 128],
                                phat[p][:, h * 256:(h + 1) * 256],
                                start=f, stop=sv))
                            started_av[bnk] = True
                        for hl in acts2:
                            fin = s_av if hl == acts2[-1] else None
                            bnk = hl // 2
                            acts_b = [a for a in acts2 if a // 2 == bnk]
                            stv = (c == L - 1 and hl == acts_b[-1])
                            t_op(lambda e, h=hl, p=pp2, f=(not started_sum[bnk]),
                                 sv=stv: e.matmul(
                                ps[6 + h // 2][0:1, (h % 2) * 256:(h % 2 + 1) * 256],
                                ones_b, phat[p][:, h * 256:(h + 1) * 256],
                                start=f, stop=sv), fin)
                            started_sum[bnk] = True

                    pend = None
                    for c in range(cmin, L):
                        pp = n_chunk % 2
                        if n_chunk >= 2:
                            w("tensor", s_px, n_chunk - 1)
                        acts = [hl for hl in range(G) if c >= c0_of(4 * g + hl, i)]
                        for hl in acts:
                            fin = s_sc if hl == acts[-1] else None
                            t_op(lambda e, h=hl, cc=c, p=pp, sll=sl, g2=g: e.matmul(
                                ps[2 * p + h // 2][:, (h % 2) * 256:(h % 2 + 1) * 256],
                                KT_sb[:, g2 * S + cc * 128:g2 * S + (cc + 1) * 128],
                                sll[h], start=True, stop=True), fin)
                        v_scd = s_sc.n
                        w("scalar", s_sc, v_scd)
                        if n_chunk >= 2:
                            w("scalar", s_av, n_chunk - 1)
                        for hl in acts:
                            h = 4 * g + hl
                            ws = subwidths(h, i)
                            qoff = 0
                            for j, Wd in enumerate(ws):
                                col = bias_col(i, g, hl, c, j)
                                fin = s_px if (hl == acts[-1] and j == len(ws) - 1) else None
                                s_op(lambda e, h2=hl, jo=qoff, Wd2=Wd, cl=col, p=pp: e.activation(
                                    phat[p][:, h2 * 256 + jo: h2 * 256 + jo + Wd2],
                                    ps[2 * p + h2 // 2][:, (h2 % 2) * 256 + jo:
                                                        (h2 % 2) * 256 + jo + Wd2],
                                    AF.Exp, bias=bias_st[:, cl:cl + 1], scale=ISQ), fin)
                                qoff += Wd
                        v_exp = s_px.n
                        masked = c >= L - 4
                        v_pm = 0
                        if masked:
                            w("vector", s_px, v_exp)
                            v_op(lambda e, cc=c, p=pp, Li=L: e.tensor_tensor(
                                phat[p], phat[p],
                                mask_st[:, (cc - (Li - 4)) * 1024:(cc - (Li - 4) + 1) * 1024],
                                ALU.mult), s_pm)
                            n_mask += 1
                            v_pm = s_pm.n
                        if pend is not None:
                            emit_av(pend)
                        pend = (c, pp, acts, masked, v_exp, v_pm)
                        n_chunk += 1
                    emit_av(pend)
                    v_avd = s_av.n
                    w("vector", s_av, v_avd)
                    vr0 = v_op(lambda e: e.reciprocal(rs_sb[0:1, 0:512], ps[6][0:1, 0:512]), ve)
                    vr1 = v_op(lambda e: e.reciprocal(row_f[0:1, 1024:1536],
                                                      ps[7][0:1, 0:512]), ve)
                    w("tensor", ve, vr1)
                    t_op(lambda e: e.matmul(ps[6][:, 0:512], onesrow, rs_sb[0:1, 0:512],
                                            start=True, stop=True))
                    vb = t_op(lambda e: e.matmul(ps[7][:, 0:512], onesrow, row_f[0:1, 1024:1536],
                                                 start=True, stop=True), te)
                    w("vector", te, vb)
                    for hl in range(G):
                        v_op(lambda e, h=hl: e.tensor_copy(
                            bc_sb[:, (h % 2) * 256:(h % 2 + 1) * 256],
                            ps[6 + h // 2][:, (h % 2) * 256:(h % 2 + 1) * 256]))
                        vsync()
                        ot = outt_t[hl]
                        if n_fin >= 4 and hl == 0:
                            w("vector", stB, stB.n)
                        vf = v_op(lambda e, h=hl, o=ot: e.scalar_tensor_tensor(
                            o, ps[4 + h // 2][:, (h % 2) * 256:(h % 2 + 1) * 256],
                            1.0, bc_sb[:, (h % 2) * 256:(h % 2 + 1) * 256],
                            ALU.mult, ALU.mult), s_fin)
                        w("sync", s_fin, vf)
                        dma("sync", stB, outT[(4 * g + hl) * 128:(4 * g + hl + 1) * 128,
                                              i * 256:(i + 1) * 256], ot)
                        n_fin += 1
            ab.pop(); af.pop()
            barrier()

        if upto >= 5:
            ab.pop()
            ab.push(); af.push()
            outT_sb = ab.take(MT * QR)
            wot_sb = [ab.take(16 * 1024)]
            x_st = [af.take(512) for _ in range(2)]
            v_st = [af.take(512) for _ in range(2)]

            w("sync", stB, stB.n)
            w("sync", gp, v_wo)
            dma("sync", ldB, outT_sb.rearrange("p (a q) -> p a q", q=QR),
                outT.rearrange("(a p) q -> p a q", p=128))
            v_ot = ldB.n
            w("tensor", ldB, v_ot)
            it = 0
            for db in range(8):
                wsl = wot_sb[0]
                if db >= 1:
                    w("sync", te, te.n)
                w("sync", ldB, ldB.n)
                dma("sync", ldB, wsl.rearrange("p (a d) -> p a d", d=512),
                    woT_bf[:, db * 512:(db + 1) * 512].rearrange("(a p) d -> p a d", p=128))
                v_w = ldB.n
                w("tensor", ldB, v_w)
                for qt in range(QT):
                    bk = it % 2
                    pA = ps[bk]
                    w("scalar", ldC, ldC.n)
                    w("scalar", ve, max(0, ve.n - 1))
                    dma("scalar", ldC, x_st[it % 2],
                        x_q[qt * 128:(qt + 1) * 128, db * 512:(db + 1) * 512])
                    if it >= 2:
                        w("tensor", ve, ve.n)
                    for cc in range(MT):
                        fin = te if cc == MT - 1 else None
                        t_op(lambda e, c2=cc, q2=qt, wl=wsl, f=(cc == 0), a=pA: e.matmul(
                            a[:, 0:512], outT_sb[:, c2 * QR + q2 * 128:c2 * QR + (q2 + 1) * 128],
                            wl[:, c2 * 512:(c2 + 1) * 512],
                            start=f, stop=(c2 == MT - 1)), fin)
                    v_mm = te.n
                    w("vector", te, v_mm)
                    w("vector", ldC, ldC.n)
                    if it >= 2:
                        w("vector", stA, stA.n)
                    vv = v_op(lambda e, b2=bk: e.tensor_tensor(
                        v_st[b2], ps[b2][:, 0:512], x_st[b2], ALU.add), ve)
                    w("sync", ve, vv)
                    dma("sync", stA, v_scr[qt * 128:(qt + 1) * 128, db * 512:(db + 1) * 512],
                        v_st[bk])
                    it += 1
            ab.pop(); af.pop()
            barrier()

        ln1_nm = asm.take(16)
        ln1_rs = asm.take(16)
        if upto >= 6:
            ab.push(); af.push()
            vta = af.take(4096)
            sq_bf6 = ab.take(4096)
            y1t_b = [ab.take(4096) for _ in range(2)]
            c_ln6 = asm.take(16)
            for qt in range(QT):
                if qt >= 1:
                    w("sync", ve, ve.n)
                    w("sync", se, se.n)
                dma("sync", ldB, vta, v_scr[qt * 128:(qt + 1) * 128, :])
                w("vector", ldB, ldB.n)
                v_op(lambda e: e.reduce_sum(c_ln6[:, 0:1], vta, axis=AX.X))
                vsync()
                vnm = v_op(lambda e, q2=qt: e.tensor_scalar(
                    ln1_nm[:, q2:q2 + 1], c_ln6[:, 0:1], -1.0 / DM, None, ALU.mult), ve)
                w("scalar", ve, vnm)
                vsq = s_op(lambda e, q2=qt: e.activation(
                    sq_bf6, vta, AF.Square, bias=ln1_nm[:, q2:q2 + 1], scale=1.0,
                    accum_out=c_ln6[:, 1:2]), se)
                w("vector", se, vsq)
                v_op(lambda e: e.tensor_scalar(c_ln6[:, 2:3], c_ln6[:, 1:2],
                                               1.0 / DM, EPS_LN, ALU.mult, ALU.add))
                vsync()
                vrc = v_op(lambda e: e.reciprocal(c_ln6[:, 3:4], c_ln6[:, 2:3]), ve)
                w("scalar", ve, vrc)
                vst = s_op(lambda e, q2=qt: e.activation(ln1_rs[:, q2:q2 + 1],
                                                         c_ln6[:, 3:4], AF.Sqrt), se)
                w("vector", se, vst)
                vsync()
                if qt >= 2:
                    w("vector", stA, stA.n)
                vy = v_op(lambda e, q2=qt: e.tensor_scalar(
                    y1t_b[q2 % 2], vta, ln1_nm[:, q2:q2 + 1], ln1_rs[:, q2:q2 + 1],
                    ALU.add, ALU.mult), ve)
                w("sync", ve, vy)
                dma("sync", stA, y1b[qt * 128:(qt + 1) * 128, :], y1t_b[qt % 2])
            ab.pop(); af.pop()
            barrier()

        if upto >= 7:
            ab.push(); af.push()
            y1_pan = [ab.take(QR) for _ in range(2)]
            for cc in range(MT):
                pan = y1_pan[cc % 2]
                if cc >= 2:
                    w("sync", stA, stA.n)
                v_ld = dma_t("sync", ldB, pan, y1b[:, cc * 128:(cc + 1) * 128])
                w("sync", ldB, ldB.n)
                dma("sync", stA, y1T[cc * 128:(cc + 1) * 128, :], pan)
            v_y1T = stA.n

            xg = [ab.take(2 * QR) for _ in range(2)]
            ub = [ab.take(2048) for _ in range(2)]
            db_sb = [ab.take(2048) for _ in range(2)]
            mid = ab.take(8 * 512)
            y2h = [ab.take(2 * QR) for _ in range(2)]

            w("gpsimd", stA, v_y1T)
            w("gpsimd", ldA, v_ldA_ph0)

            def ffn_loads(h):
                bb = h % 2
                dma("gpsimd", gp, ub[bb].rearrange("p (ch m) -> p ch m", ch=2),
                    upw[h].rearrange("(ch p) m -> p ch m", p=128))
                dma("gpsimd", gp, db_sb[bb].rearrange("p (mc d) -> p mc d", mc=8),
                    downw[h].rearrange("(mc p) d -> p mc d", p=128))
                v1 = gp.inc(16)
                prog.add("gpsimd", lambda e, h2=h: e.indirect_dma_start(
                    out=xg[h2 % 2][:, 0:QR], out_offset=None, in_=y1T,
                    in_offset=IndirectOffsetOnAxis(perm_sb[:, 2 * h2:2 * h2 + 1], 0)
                ).then_inc(gp.h, 16))
                v2 = gp.inc(16)
                prog.add("gpsimd", lambda e, h2=h: e.indirect_dma_start(
                    out=xg[h2 % 2][:, QR:2 * QR], out_offset=None, in_=y1T,
                    in_offset=IndirectOffsetOnAxis(perm_sb[:, 2 * h2 + 1:2 * h2 + 2], 0)
                ).then_inc(gp.h, 16))
                return gp.n

            v_ld = [None] * 17
            v_ld[0] = ffn_loads(0)
            for h in range(16):
                w("tensor", gp, v_ld[h])
                if h < 15:
                    w("gpsimd", te, te.n)
                    w("gpsimd", gp, gp.n)
                    v_ld[h + 1] = ffn_loads(h + 1)
                if h >= 2:
                    w("vector", gsc, gsc.n)
                for qb in range(4):
                    itq = h * 4 + qb
                    w("tensor", se, se.n)
                    w("tensor", ve, ve.n)
                    w("scalar", te, te.n)
                    for mt in range(8):
                        for ch in range(2):
                            fin = te if (mt == 7 and ch == 1) else None
                            t_op(lambda e, m2=mt, c2=ch, b2=h % 2, q2=qb: e.matmul(
                                ps[m2][:, 0:512],
                                ub[b2][:, c2 * 1024 + m2 * 128:c2 * 1024 + (m2 + 1) * 128],
                                xg[b2][:, c2 * QR + q2 * 512:c2 * QR + (q2 + 1) * 512],
                                start=(c2 == 0), stop=(c2 == 1)), fin)
                    v_up = te.n
                    w("scalar", te, v_up)
                    for mt in range(8):
                        fin = se if mt == 7 else None
                        s_op(lambda e, m2=mt: e.activation(
                            mid[:, m2 * 512:(m2 + 1) * 512], ps[m2][:, 0:512],
                            AF.Gelu), fin)
                    v_gel = se.n
                    w("tensor", se, v_gel)
                    for dt in range(2):
                        for mc2 in range(8):
                            fin = te if (dt == 1 and mc2 == 7) else None
                            t_op(lambda e, d2=dt, m2=mc2, b2=h % 2: e.matmul(
                                ps[d2][:, 0:512],
                                db_sb[b2][:, m2 * 256 + d2 * 128:m2 * 256 + (d2 + 1) * 128],
                                mid[:, m2 * 512:(m2 + 1) * 512],
                                start=(m2 == 0), stop=(m2 == 7)), fin)
                    v_dn = te.n
                    w("vector", te, v_dn)
                    for dt in range(2):
                        fin = ve if dt == 1 else None
                        v_op(lambda e, d2=dt, b2=h % 2, q2=qb: e.tensor_copy(
                            y2h[b2][:, d2 * QR + q2 * 512:d2 * QR + (q2 + 1) * 512],
                            ps[d2][:, 0:512]), fin)
                v_ev = ve.n
                w("gpsimd", ve, v_ev)
                for t2 in range(2):
                    gsc.inc(16)
                    prog.add("gpsimd", lambda e, h2=h, t3=t2: e.indirect_dma_start(
                        out=y2T, out_offset=IndirectOffsetOnAxis(
                            perm_sb[:, 2 * h2 + t3:2 * h2 + t3 + 1], 0),
                        in_=y2h[h2 % 2][:, t3 * QR:(t3 + 1) * QR], in_offset=None
                    ).then_inc(gsc.h, 16))
            ab.pop(); af.pop()
            barrier()

        if upto >= 8:
            ab.push(); af.push()
            vt_f = af.take(4096)
            y2_b = ab.take(4096)
            sq_bf8 = ab.take(4096)
            c_ln8 = asm.take(8)
            w("sync", gsc, gsc.n)
            for qt in range(QT):
                if qt >= 1:
                    w("sync", stB, stB.n)
                    w("sync", se, se.n)
                v_y2 = dma_t("sync", ldB, y2_b, y2T[:, qt * 128:(qt + 1) * 128])
                v_ld = dma("sync", ldB, vt_f, v_scr[qt * 128:(qt + 1) * 128, :])
                w("vector", ldB, ldB.n)
                v_op(lambda e, q2=qt: e.tensor_scalar(
                    vt_f, vt_f, ln1_nm[:, q2:q2 + 1], ln1_rs[:, q2:q2 + 1],
                    ALU.add, ALU.mult))
                vsync()
                v_op(lambda e: e.tensor_tensor(vt_f, vt_f, y2_b, ALU.add))
                vsync()
                v_op(lambda e: e.reduce_sum(c_ln8[:, 0:1], vt_f, axis=AX.X))
                vsync()
                vnm = v_op(lambda e: e.tensor_scalar(c_ln8[:, 1:2], c_ln8[:, 0:1],
                                                     -1.0 / DM, None, ALU.mult), ve)
                w("scalar", ve, vnm)
                vsq = s_op(lambda e: e.activation(sq_bf8, vt_f, AF.Square,
                                                  bias=c_ln8[:, 1:2], scale=1.0,
                                                  accum_out=c_ln8[:, 2:3]), se)
                w("vector", se, vsq)
                v_op(lambda e: e.tensor_scalar(c_ln8[:, 3:4], c_ln8[:, 2:3],
                                               1.0 / DM, EPS_LN, ALU.mult, ALU.add))
                vsync()
                vrc = v_op(lambda e: e.reciprocal(c_ln8[:, 4:5], c_ln8[:, 3:4]), ve)
                w("scalar", ve, vrc)
                vst = s_op(lambda e: e.activation(c_ln8[:, 5:6], c_ln8[:, 4:5], AF.Sqrt), se)
                w("vector", se, vst)
                vsync()
                vy = v_op(lambda e: e.tensor_scalar(
                    vt_f, vt_f, c_ln8[:, 1:2], c_ln8[:, 5:6], ALU.add, ALU.mult), ve)
                w("sync", ve, vy)
                dma("sync", stB, out[qt * 128:(qt + 1) * 128, :], vt_f)
            ab.pop()

        for e in ENGS:
            for c in (stA, stB, gp, ldA, ldB, ldC):
                if c.n:
                    prog.add(e, lambda eng, s=c.h, v=c.n: eng.wait_ge(s, v))

        def run_eng(name):
            def body(eng):
                for fn in prog.items[name]:
                    fn(eng)
            return body

        block.sync(run_eng("sync"))
        block.tensor(run_eng("tensor"))
        block.vector(run_eng("vector"))
        block.scalar(run_eng("scalar"))
        block.gpsimd(run_eng("gpsimd"))

    return nc



def make_inputs(core, full):
    b, half = core // 2, core % 2
    ranges = core_ranges(core)
    x = np.asarray(full["L0_attn_x"])
    x_q = np.concatenate([x[b, 256 * r:256 * r + 256, :] for r in ranges], axis=0)
    bias, mask = build_tables(core)
    cache = full.setdefault("_t_cache", {})
    def tr(key, arr):
        if key not in cache:
            cache[key] = np.ascontiguousarray(np.asarray(arr).T)
        return cache[key]
    return {
        "x_q": np.ascontiguousarray(x_q),
        "x_qT": np.ascontiguousarray(x_q.T),
        "x_kvT": tr(("xkv", b), x[b]),
        "WqT": tr("wq", full["L0_attn_Wq"]),
        "WkT": tr("wk", full["L0_attn_Wk"]),
        "WvT": tr("wv", full["L0_attn_Wv"]),
        "WoT": tr("wo", full["L0_attn_Wo"]),
        "qw": np.asarray(full["L0_attn_q_norm_w"]).reshape(1, HD),
        "kw": np.asarray(full["L0_attn_k_norm_w"]).reshape(1, HD),
        "upw": np.asarray(full["L0_ffn_up"]),
        "downw": np.asarray(full["L0_ffn_down"]),
        "perm": np.asarray(full["L0_ffn_perm"]).astype(np.int32),
        "bias_tab": bias,
        "mask_tab": mask,
        "ones_row": np.ones((1, 128), dtype=np.float32),
    }


def kernel(**inputs):
    nc = build_program()
    in_maps = [make_inputs(c, inputs) for c in range(NCORE)]
    res = run_bass_kernel_spmd(nc, in_maps, core_ids=list(range(NCORE)))
    y = np.empty((B, S, DM), dtype=np.float32)
    for c in range(NCORE):
        o = res.results[c]["out"]
        for i, r in enumerate(core_ranges(c)):
            y[c // 2, 256 * r:256 * r + 256, :] = o[256 * i:256 * (i + 1), :]
    return y


# version 11
# speedup vs baseline: 1.6142x; 1.6142x over previous
import math
from contextlib import ExitStack

import numpy as np
import ml_dtypes

import concourse.bass as bass
import concourse.mybir as mybir
from concourse.bass_core_rust import IndirectOffsetOnAxis
from concourse.bass_utils import run_bass_kernel_spmd

F32 = mybir.dt.float32
BF16 = mybir.dt.bfloat16
I32 = mybir.dt.int32
AF = mybir.ActivationFunctionType
ALU = mybir.AluOpType
AX = mybir.AxisListType

B, S, DM = 4, 4096, 4096
HKV, G, HD = 8, 4, 128
NH = HKV * G
QR = 2048
NSLOT, SLOT_Q = 8, 256
L_SLOT = [4 * (i + 1) for i in range(NSLOT)]
SLOPES = [2.0 ** (-(h + 1) / 4.0) for h in range(NH)]
W_HEAD = [64] * 4 + [128] * 4 + [256] * 24
KEEP_H = [min(32, 4 + math.ceil(2.0 / SLOPES[h])) for h in range(NH)]
SHIFT = 10.0
ISQ = 1.0 / math.sqrt(HD)
NEG = -1e30
MT = DM // 128
QT = QR // 128
NCORE = 8
EPS_RMS = 1e-6
EPS_LN = 1e-5


def subwidths(h, i):
    if i == 0:
        if h < 4:
            return [16, 16, 16, 16, 32, 32, 64, 64]
        if h < 8:
            return [32, 32, 64, 128]
        if h < 12:
            return [128, 128]
    W = W_HEAD[h]
    return [W] * (SLOT_Q // W)


def c0_of(h, i):
    return max(0, L_SLOT[i] - KEEP_H[h])


def bias_layout():
    off, tot = {}, 0
    for i in range(NSLOT):
        for g in range(HKV):
            off[(i, g)] = tot
            for hl in range(G):
                h = 4 * g + hl
                tot += (L_SLOT[i] - c0_of(h, i)) * len(subwidths(h, i))
    return tot, off


BIAS_TOT, BIAS_OFF = bias_layout()


def bias_col(i, g, hl, c, j):
    h = 4 * g + hl
    idx = 0
    for hl2 in range(hl):
        h2 = 4 * g + hl2
        idx += (L_SLOT[i] - c0_of(h2, i)) * len(subwidths(h2, i))
    idx += (c - c0_of(h, i)) * len(subwidths(h, i)) + j
    return idx


def core_ranges(core):
    half = core % 2
    return [2 * i + 1 - half for i in range(NSLOT)]


def build_tables(core):
    ranges = core_ranges(core)
    bias = np.empty((128, BIAS_TOT), dtype=np.float32)
    p = np.arange(128, dtype=np.float64)
    for i in range(NSLOT):
        for g in range(HKV):
            base = BIAS_OFF[(i, g)]
            for hl in range(G):
                h = 4 * g + hl
                ws = subwidths(h, i)
                q_ends = np.cumsum(ws) + 256 * ranges[i]
                for c in range(c0_of(h, i), L_SLOT[i]):
                    s_abs = 128 * c + p
                    for j, q_end in enumerate(q_ends):
                        col = base + bias_col(i, g, hl, c, j)
                        v = SLOPES[h] * (s_abs - q_end) - SHIFT
                        bias[:, col] = np.where(s_abs >= q_end, NEG, v).astype(np.float32)
    mask = np.zeros((NSLOT, 4, 128, 4 * SLOT_Q), dtype=np.float32)
    for i in range(NSLOT):
        L = L_SLOT[i]
        r = ranges[i]
        for mm in range(4):
            s_abs = 128 * (L - 4 + mm) + np.arange(128)[:, None]
            q_abs = 256 * r + np.arange(SLOT_Q)[None, :]
            mask[i, mm] = np.tile((s_abs <= q_abs).astype(np.float32), (1, 4))
    return bias, mask.astype(ml_dtypes.bfloat16)


ENGS = ("sync", "tensor", "vector", "scalar", "gpsimd")


class Prog:
    def __init__(self):
        self.items = {e: [] for e in ENGS}

    def add(self, eng, fn):
        self.items[eng].append(fn)


class Ctr:
    def __init__(self, handle, name=""):
        self.h, self.n, self.name = handle, 0, name

    def inc(self, k=1):
        self.n += k
        return self.n


class Alloc:
    def __init__(self, tensor, n, name):
        self.t, self.n, self.off, self.name = tensor, n, 0, name
        self.marks, self.hi = [], 0

    def take(self, cols, align=32):
        self.off = (self.off + align - 1) // align * align
        assert self.off + cols <= self.n, f"{self.name} overflow {self.off}+{cols}>{self.n}"
        ap = self.t[:, self.off:self.off + cols]
        self.off += cols
        self.hi = max(self.hi, self.off)
        return ap

    def push(self):
        self.marks.append(self.off)

    def pop(self):
        self.off = self.marks.pop()


def build_program():
    nc = bass.Bass("TRN2", target_bir_lowering=False)

    x_q = nc.dram_tensor("x_q", [QR, DM], F32, kind="ExternalInput").ap()
    x_qT = nc.dram_tensor("x_qT", [DM, QR], F32, kind="ExternalInput").ap()
    x_kvT = nc.dram_tensor("x_kvT", [DM, S], F32, kind="ExternalInput").ap()
    WqT = nc.dram_tensor("WqT", [DM, DM], F32, kind="ExternalInput").ap()
    WkT = nc.dram_tensor("WkT", [DM, 1024], F32, kind="ExternalInput").ap()
    WvT = nc.dram_tensor("WvT", [DM, 1024], F32, kind="ExternalInput").ap()
    WoT = nc.dram_tensor("WoT", [DM, DM], F32, kind="ExternalInput").ap()
    qw = nc.dram_tensor("qw", [1, HD], F32, kind="ExternalInput").ap()
    kw = nc.dram_tensor("kw", [1, HD], F32, kind="ExternalInput").ap()
    upw = nc.dram_tensor("upw", [16, 256, 1024], F32, kind="ExternalInput").ap()
    downw = nc.dram_tensor("downw", [16, 1024, 256], F32, kind="ExternalInput").ap()
    perm = nc.dram_tensor("perm", [DM], I32, kind="ExternalInput").ap()
    bias_tab = nc.dram_tensor("bias_tab", [128, BIAS_TOT], F32, kind="ExternalInput").ap()
    mask_tab = nc.dram_tensor("mask_tab", [NSLOT, 4, 128, 4 * SLOT_Q], BF16, kind="ExternalInput").ap()
    ones_row = nc.dram_tensor("ones_row", [1, 128], F32, kind="ExternalInput").ap()

    out = nc.dram_tensor("out", [QR, DM], F32, kind="ExternalOutput").ap()

    khat = nc.dram_tensor("khat", [S, 1024], BF16, kind="Internal").ap()
    qhat = nc.dram_tensor("qhat", [QR, DM], BF16, kind="Internal").ap()
    outT = nc.dram_tensor("outT", [DM, QR], BF16, kind="Internal").ap()
    v_scr = nc.dram_tensor("v_scr", [QR, DM], F32, kind="Internal").ap()
    y1b = nc.dram_tensor("y1b", [QR, DM], BF16, kind="Internal").ap()
    y1T = nc.dram_tensor("y1T", [DM, QR], BF16, kind="Internal").ap()
    y2T = nc.dram_tensor("y2T", [DM, QR], BF16, kind="Internal").ap()

    prog = Prog()

    with ExitStack() as es:
        ec = es.enter_context
        BFN, F3N, SMN = 88 * 1024, 4608, 1536
        ab = Alloc(ec(nc.sbuf_tensor("arena_bf", [128, BFN], BF16)), BFN, "bf")
        af = Alloc(ec(nc.sbuf_tensor("arena_f32", [128, F3N], F32)), F3N, "f32")
        asm = Alloc(ec(nc.sbuf_tensor("arena_small", [128, SMN], F32)), SMN, "small")
        ai = Alloc(ec(nc.sbuf_tensor("arena_i32", [128, 64], I32)), 64, "i32")
        row_f = ec(nc.sbuf_tensor("row_f", [1, 1536], F32))

        ps = [ec(nc.psum_tensor(f"ps{i}", [128, 512], F32)) for i in range(8)]

        names = ("ldA", "ldB", "ldC", "ldD", "te", "ve", "se", "gp", "gsc",
                 "stA", "stB", "stC", "s_sc", "s_px", "s_pm", "s_av", "s_fin")
        sems = {n: Ctr(ec(nc.semaphore(n)), n) for n in names}
        ldA, ldB, ldC = sems["ldA"], sems["ldB"], sems["ldC"]
        te, ve, se, gp = sems["te"], sems["ve"], sems["se"], sems["gp"]
        stA, stB = sems["stA"], sems["stB"]
        s_sc, s_px, s_pm = sems["s_sc"], sems["s_px"], sems["s_pm"]
        s_av, s_fin = sems["s_av"], sems["s_fin"]
        gsc = sems["gsc"]
        ldD, stC = sems["ldD"], sems["stC"]

        block = ec(nc.Block())

        def w(eng, sem, val):
            if val > 0:
                prog.add(eng, lambda e, s=sem.h, v=val: e.wait_ge(s, v))

        def dma(eng, sem, dst, src, **kw):
            v = sem.inc(16)
            prog.add(eng, lambda e, d=dst, s=src, k=dict(kw):
                     e.dma_start(out=d, in_=s, **k).then_inc(sem.h, 16))
            return v

        def dma_t(eng, sem, dst, src):
            v = sem.inc(16)
            prog.add(eng, lambda e, d=dst, s=src:
                     e.dma_start(out=d, in_=s, transpose=True).then_inc(sem.h, 16))
            return v

        def barrier():
            vals = [(c.h, c.n) for c in sems.values() if c.n > 0]
            for e in ENGS:
                for hh, v in vals:
                    prog.add(e, lambda eng, s=hh, vv=v: eng.wait_ge(s, vv))

        def t_op(fn, inc=None):
            if inc is None:
                prog.add("tensor", fn)
                return None
            v = inc.inc(1)
            prog.add("tensor", lambda e, f=fn: f(e).then_inc(inc.h, 1))
            return v

        def v_op(fn, inc=None):
            c = ve if inc is None else inc
            v = c.inc(1)
            prog.add("vector", lambda e, f=fn, cc=c: f(e).then_inc(cc.h, 1))
            return v

        def s_op(fn, inc=None):
            c = se if inc is None else inc
            v = c.inc(1)
            prog.add("scalar", lambda e, f=fn, cc=c: f(e).then_inc(cc.h, 1))
            return v

        def vsync():
            w("vector", ve, ve.n)

        ones_b = ab.take(32)[:, 0:1]
        wq_t = af.take(128)
        wk_t = af.take(128)
        onesrow = row_f[0:1, 0:128]
        qw_sb = row_f[0:1, 128:256]
        kw_sb = row_f[0:1, 256:384]
        perm_sb = ai.take(32)

        prog.add("vector", lambda e: e.memset(ones_b, 1.0))
        dma("sync", ldA, perm_sb, perm.rearrange("(t p) -> p t", p=128), allow_slow_non_contiguous=True)
        dma("sync", ldA, onesrow, ones_row)
        dma("sync", ldA, qw_sb, qw)
        dma("sync", ldA, kw_sb, kw)
        v_ldA_ph0 = ldA.n
        w("tensor", ldA, ldA.n)
        t_op(lambda e: e.matmul(ps[7][:, 0:128], onesrow, qw_sb, start=True, stop=True))
        vt = t_op(lambda e: e.matmul(ps[7][:, 128:256], onesrow, kw_sb, start=True, stop=True), te)
        w("vector", te, vt)
        v_op(lambda e: e.tensor_copy(wq_t, ps[7][:, 0:128]))
        v_op(lambda e: e.tensor_copy(wk_t, ps[7][:, 128:256]), ve)

        def rms_evac(pA, pB, sq, cols, wnorm_t, ot, store_gate):
            w("scalar", te, te.n)
            if store_gate is not None:
                w("vector", stA, store_gate)
            s_op(lambda e, a=pA: e.activation(sq[:, 0:512], a[:, 0:512], AF.Square))
            vsq = s_op(lambda e, b=pB: e.activation(sq[:, 512:1024], b[:, 0:512], AF.Square), se)
            w("vector", se, vsq)
            for hh in range(8):
                v_op(lambda e, h=hh: e.reduce_sum(
                    cols[:, h:h + 1], sq[:, h * 128:(h + 1) * 128], axis=AX.X))
            vsync()
            v_op(lambda e: e.tensor_scalar(cols[:, 8:16], cols[:, 0:8],
                                           1.0 / HD, EPS_RMS, ALU.mult, ALU.add))
            vsync()
            vrec = v_op(lambda e: e.reciprocal(cols[:, 16:24], cols[:, 8:16]), ve)
            w("scalar", ve, vrec)
            vsq2 = s_op(lambda e: e.activation(cols[:, 24:32], cols[:, 16:24], AF.Sqrt), se)
            w("vector", se, vsq2)
            for hh in range(8):
                pp_ = pA if hh < 4 else pB
                off = (hh % 4) * 128
                fin = ve if hh == 7 else None
                v_op(lambda e, p=pp_, o=off, h=hh, k=ot: e.scalar_tensor_tensor(
                    k[:, h * 128:(h + 1) * 128], p[:, o:o + 128],
                    cols[:, 24 + h:25 + h], wnorm_t,
                    ALU.mult, ALU.mult), fin)
            w("sync", ve, ve.n)

        ab.push(); af.push()
        wqT_h = ab.take(MT * 2048)
        xq_stage = [ab.take(MT * 128) for _ in range(2)]
        qtile = [ab.take(1024) for _ in range(2)]
        sq_q = af.take(1024)
        cols_q = asm.take(64)

        tile_te = {}
        ld_for = {}
        ve_q, st_q = [], []

        def stage_qt(qt, prev_te):
            if prev_te is not None:
                w("gpsimd", te, prev_te)
            dma("gpsimd", ldC, xq_stage[qt % 2].rearrange("p (a q) -> p a q", q=128),
                x_qT[:, qt * 128:(qt + 1) * 128].rearrange("(a p) q -> p a q", p=128))
            return ldC.n

        it = 0
        for dh in range(2):
            if dh == 1:
                w("gpsimd", te, tile_te[15])
            dma("gpsimd", ldB, wqT_h.rearrange("p (a d) -> p a d", d=2048),
                WqT[:, dh * 2048:(dh + 1) * 2048].rearrange("(a p) d -> p a d", p=128))
            v_wqT = ldB.n
            for qt in range(QT):
                gidx = dh * 16 + qt
                if dh == 0 and qt == 0:
                    ld_for[0] = stage_qt(0, None)
                if qt + 1 < QT:
                    ld_for[dh * 16 + qt + 1] = stage_qt(qt + 1, tile_te.get(dh * 16 + qt - 1))
                elif dh == 0:
                    ld_for[16] = stage_qt(0, tile_te.get(14))
                stg = xq_stage[qt % 2]
                w("tensor", ldC, ld_for[gidx])
                if qt == 0:
                    w("tensor", ldB, v_wqT)
                for dq2 in range(2):
                    dq = 2 * dh + dq2
                    bk = it % 2
                    pA, pB = ps[2 * bk], ps[2 * bk + 1]
                    if it >= 2:
                        w("tensor", ve, ve_q[it - 2])
                    for mc in range(MT):
                        lt = stg[:, mc * 128:(mc + 1) * 128]
                        first = (mc == 0)
                        t_op(lambda e, l=lt, m=mc, f=first, a=pA, d2=dq2: e.matmul(
                            a[:, 0:512], l,
                            wqT_h[:, m * 2048 + d2 * 1024: m * 2048 + d2 * 1024 + 512],
                            start=f, stop=(m == MT - 1)))
                        fin = te if mc == MT - 1 else None
                        t_op(lambda e, l=lt, m=mc, f=first, b=pB, d2=dq2: e.matmul(
                            b[:, 0:512], l,
                            wqT_h[:, m * 2048 + d2 * 1024 + 512: m * 2048 + (d2 + 1) * 1024],
                            start=f, stop=(m == MT - 1)), fin)
                    ot = qtile[it % 2]
                    rms_evac(pA, pB, sq_q, cols_q, wq_t, ot,
                             st_q[it - 2] if it >= 2 else None)
                    dma("sync", stA, qhat[qt * 128:(qt + 1) * 128,
                                          dq * 1024:(dq + 1) * 1024], ot)
                    st_q.append(stA.n)
                    ve_q.append(ve.n)
                    it += 1
                tile_te[gidx] = te.n
        ab.pop(); af.pop()
        barrier()

        ab.push()
        V_sb = ab.take(MT * 1024)
        ab.push(); af.push()
        WT_sb = ab.take(MT * 1024)
        pan_sb = ab.take(MT * 512)
        ktile = [ab.take(1024) for _ in range(2)]
        sq_k = af.take(1024)
        cols_k = asm.take(64)

        dma("gpsimd", ldB, WT_sb.rearrange("p (a d) -> p a d", d=1024),
            WkT.rearrange("(a p) d -> p a d", p=128))
        v_wkT = ldB.n

        pan3 = pan_sb.rearrange("p (a s) -> p a s", s=512)

        def stage_half(H, gate_te):
            if gate_te is not None:
                w("gpsimd", te, gate_te)
            hp = H % 2
            dma("gpsimd", ldB, pan3[:, :, hp * 256:(hp + 1) * 256],
                x_kvT.rearrange("(a p) s -> p a s", p=128)[:, :, H * 256:(H + 1) * 256])
            return ldB.n

        def cur_stage_k(t, mc):
            st = t % 4
            return pan_sb[:, mc * 512 + st * 128: mc * 512 + (st + 1) * 128]

        te_k, ve_k, st_k = [], [], []
        ld_half = {}
        ld_half[0] = stage_half(0, None)
        ld_half[1] = stage_half(1, None)
        for t in range(32):
            if t % 2 == 0 and t > 0 and t // 2 + 1 < 16:
                H = t // 2 + 1
                ld_half[H] = stage_half(H, te_k[-1])
            bk = t % 2
            pA, pB = ps[2 * bk], ps[2 * bk + 1]
            w("tensor", ldB, max(ld_half[t // 2], v_wkT))
            if t >= 2:
                w("tensor", ve, ve_k[t - 2])
            for mc in range(MT):
                lt = cur_stage_k(t, mc)
                first = (mc == 0)
                t_op(lambda e, l=lt, m=mc, f=first, a=pA: e.matmul(
                    a[:, 0:512], l, WT_sb[:, m * 1024:m * 1024 + 512],
                    start=f, stop=(m == MT - 1)))
                fin = te if mc == MT - 1 else None
                t_op(lambda e, l=lt, m=mc, f=first, b=pB: e.matmul(
                    b[:, 0:512], l, WT_sb[:, m * 1024 + 512:(m + 1) * 1024],
                    start=f, stop=(m == MT - 1)), fin)
            te_k.append(te.n)
            ot = ktile[t % 2]
            rms_evac(pA, pB, sq_k, cols_k, wk_t, ot,
                     st_k[t - 2] if t >= 2 else None)
            dma("sync", stA, khat[t * 128:(t + 1) * 128, :], ot)
            st_k.append(stA.n)
            ve_k.append(ve.n)

        w("gpsimd", te, te_k[-1])
        dma("gpsimd", ldB, WT_sb.rearrange("p (a d) -> p a d", d=1024),
            WvT.rearrange("(a p) d -> p a d", p=128))
        v_wvT = ldB.n

        te_v, ve_v = [], []
        ld_half_v = {}
        ld_half_v[0] = stage_half(0, None)
        ld_half_v[1] = stage_half(1, None)
        for t in range(32):
            if t % 2 == 0 and t > 0 and t // 2 + 1 < 16:
                H = t // 2 + 1
                ld_half_v[H] = stage_half(H, te_v[-1])
            bk = t % 2
            pA, pB = ps[2 * bk], ps[2 * bk + 1]
            w("tensor", ldB, max(ld_half_v[t // 2], v_wvT))
            if t >= 2:
                w("tensor", ve, ve_v[t - 2])
            for mc in range(MT):
                lt = cur_stage_k(t, mc)
                first = (mc == 0)
                t_op(lambda e, l=lt, m=mc, f=first, a=pA: e.matmul(
                    a[:, 0:512], l, WT_sb[:, m * 1024:m * 1024 + 512],
                    start=f, stop=(m == MT - 1)))
                fin = te if mc == MT - 1 else None
                t_op(lambda e, l=lt, m=mc, f=first, b=pB: e.matmul(
                    b[:, 0:512], l, WT_sb[:, m * 1024 + 512:(m + 1) * 1024],
                    start=f, stop=(m == MT - 1)), fin)
            te_v.append(te.n)
            v_mm = te.n
            w("vector", te, v_mm)
            v_op(lambda e, sg=t, a=pA: e.tensor_copy(V_sb[:, sg * 1024: sg * 1024 + 512],
                                                     a[:, 0:512]))
            v_op(lambda e, sg=t, b=pB: e.tensor_copy(V_sb[:, sg * 1024 + 512:(sg + 1) * 1024],
                                                     b[:, 0:512]), ve)
            ve_v.append(ve.n)
        ab.pop(); af.pop()
        barrier()

        ab.push(); af.push()
        KT_sb = ab.take(HKV * S)
        qT_sl = [ab.take(1024) for _ in range(2)]
        phat = [ab.take(4 * 256) for _ in range(2)]
        mask_st = ab.take(4 * 1024)
        outt_t = [ab.take(256) for _ in range(4)]
        bias_st = [af.take(1024) for _ in range(2)]
        bc_sb = af.take(512)
        rs_sb = row_f[0:1, 512:1024]

        w("sync", stA, stA.n)
        for kh in range(HKV):
            dma_t("sync", ldB, KT_sb[:, kh * S:(kh + 1) * S],
                  khat[:, kh * 128:(kh + 1) * 128])
        v_kt = ldB.n
        w("tensor", ldB, v_kt)

        n_chunk, n_qld = 0, 0
        n_fin_c = [0]
        ssc_grp, spx_grp = [], []
        bank_px = {}
        phat_sum = {}
        pending_fin = [None]

        def flush_fin():
            if pending_fin[0] is not None:
                pending_fin[0]()
                pending_fin[0] = None

        def make_finalize(i2, g2, v_avd):
            def emit_fin():
                w("vector", s_av, v_avd)
                v_op(lambda e: e.reciprocal(rs_sb[0:1, 0:512], ps[6][0:1, 0:512]), ve)
                vr1 = v_op(lambda e: e.reciprocal(row_f[0:1, 1024:1536],
                                                  ps[7][0:1, 0:512]), ve)
                w("tensor", ve, vr1)
                t_op(lambda e: e.matmul(ps[6][:, 0:512], onesrow, rs_sb[0:1, 0:512],
                                        start=True, stop=True))
                vb = t_op(lambda e: e.matmul(ps[7][:, 0:512], onesrow,
                                             row_f[0:1, 1024:1536],
                                             start=True, stop=True), te)
                w("vector", te, vb)
                for hl in range(G):
                    v_op(lambda e, h=hl: e.tensor_copy(
                        bc_sb[:, (h % 2) * 256:(h % 2 + 1) * 256],
                        ps[6 + h // 2][:, (h % 2) * 256:(h % 2 + 1) * 256]))
                    vsync()
                    ot = outt_t[hl]
                    if n_fin_c[0] >= 4 and hl == 0:
                        w("vector", stB, stB.n)
                    vf = v_op(lambda e, h=hl, o=ot: e.scalar_tensor_tensor(
                        o, ps[4 + h // 2][:, (h % 2) * 256:(h % 2 + 1) * 256],
                        1.0, bc_sb[:, (h % 2) * 256:(h % 2 + 1) * 256],
                        ALU.mult, ALU.mult), s_fin)
                    w("sync", s_fin, vf)
                    dma("sync", stB, outT[(4 * g2 + hl) * 128:(4 * g2 + hl + 1) * 128,
                                          i2 * 256:(i2 + 1) * 256], ot)
                    n_fin_c[0] += 1
            return emit_fin

        for i in range(NSLOT):
            L = L_SLOT[i]
            w("sync", ve, ve.n)
            dma("sync", ldA, mask_st.rearrange("p (c q) -> p c q", c=4),
                mask_tab[i].rearrange("c p q -> p c q"))
            v_mst = ldA.n
            for g in range(HKV):
                j = n_qld
                ncols = sum((L - c0_of(4 * g + hl, i)) * len(subwidths(4 * g + hl, i))
                            for hl in range(G))
                bst = bias_st[j % 2]
                if j >= 2:
                    w("scalar", s_px, spx_grp[j - 2])
                v_bst = dma("scalar", ldC, bst[:, 0:ncols],
                            bias_tab[:, BIAS_OFF[(i, g)]:BIAS_OFF[(i, g)] + ncols])
                sl = qT_sl[j % 2]
                if j >= 2:
                    w("sync", s_sc, ssc_grp[j - 2])
                for hl in range(G):
                    dma_t("sync", ldB, sl[:, hl * 256:(hl + 1) * 256],
                          qhat[i * 256:(i + 1) * 256,
                               (4 * g + hl) * 128:(4 * g + hl + 1) * 128])
                v_qsl = ldB.n
                n_qld += 1

                cmin = min(c0_of(4 * g + hl, i) for hl in range(G))
                started_av = [False] * G
                started_sum = [False] * G
                first_av_emitted = [False]
                w("tensor", ldB, v_qsl)
                w("scalar", ldC, v_bst)
                if g == 0:
                    w("vector", ldA, v_mst)

                def emit_av(pend):
                    c, pp2, acts2, mskd, v_exp2, v_pm2 = pend
                    if not first_av_emitted[0]:
                        flush_fin()
                        w("tensor", s_fin, s_fin.n)
                        first_av_emitted[0] = True
                    if mskd:
                        w("tensor", s_pm, v_pm2)
                    else:
                        w("tensor", s_px, v_exp2)
                    groups = []
                    for pi in (0, 1):
                        pr = [hl for hl in (2 * pi, 2 * pi + 1) if hl in acts2]
                        if not pr:
                            continue
                        if len(pr) == 2 and started_av[pr[0]] == started_av[pr[1]]:
                            groups.append((pi, pr))
                        else:
                            for hl in pr:
                                groups.append((pi, [hl]))
                    for pi, pr in groups:
                        off = 0 if len(pr) == 2 else (pr[0] % 2) * 256
                        wdt = 512 if len(pr) == 2 else 256
                        strt = not started_av[pr[0]]
                        stv = (c == L - 1)
                        t_op(lambda e, p2=pi, o=off, wd=wdt, cc=c, p=pp2, f=strt,
                             sv=stv, g2=g: e.matmul(
                            ps[4 + p2][:, o:o + wd],
                            V_sb[:, cc * 1024 + g2 * 128: cc * 1024 + (g2 + 1) * 128],
                            phat[p][:, p2 * 512 + o: p2 * 512 + o + wd],
                            start=f, stop=sv))
                        for hl in pr:
                            started_av[hl] = True
                    for gi2, (pi, pr) in enumerate(groups):
                        off = 0 if len(pr) == 2 else (pr[0] % 2) * 256
                        wdt = 512 if len(pr) == 2 else 256
                        strt = not started_sum[pr[0]]
                        stv = (c == L - 1)
                        fin = s_av if gi2 == len(groups) - 1 else None
                        t_op(lambda e, p2=pi, o=off, wd=wdt, p=pp2, f=strt,
                             sv=stv: e.matmul(
                            ps[6 + p2][0:1, o:o + wd], ones_b,
                            phat[p][:, p2 * 512 + o: p2 * 512 + o + wd],
                            start=f, stop=sv), fin)
                        for hl in pr:
                            started_sum[hl] = True

                pend = None
                for c in range(cmin, L):
                    pp = n_chunk % 2
                    if n_chunk >= 2:
                        w("tensor", s_px, n_chunk - 1)
                    acts = [hl for hl in range(G) if c >= c0_of(4 * g + hl, i)]
                    sgrp = []
                    for pi in (0, 1):
                        pr = [hl for hl in (2 * pi, 2 * pi + 1) if hl in acts]
                        if not pr:
                            continue
                        if len(pr) == 2:
                            sgrp.append((pi, 0, 512))
                        else:
                            sgrp.append((pi, (pr[0] % 2) * 256, 256))
                    for gi2, (pi, off, wdt) in enumerate(sgrp):
                        fin = s_sc if gi2 == len(sgrp) - 1 else None
                        t_op(lambda e, p2=pi, o=off, wd=wdt, cc=c, p=pp, g2=g,
                             sll=sl: e.matmul(
                            ps[2 * p + p2][:, o:o + wd],
                            KT_sb[:, g2 * S + cc * 128:g2 * S + (cc + 1) * 128],
                            sll[:, p2 * 512 + o: p2 * 512 + o + wd],
                            start=True, stop=True), fin)
                    v_scd = s_sc.n
                    w("scalar", s_sc, v_scd)
                    if n_chunk >= 2:
                        w("scalar", s_av, n_chunk - 1)
                    for hl in acts:
                        h = 4 * g + hl
                        ws = subwidths(h, i)
                        qoff = 0
                        for jj, Wd in enumerate(ws):
                            col = bias_col(i, g, hl, c, jj)
                            fin = s_px if (hl == acts[-1] and jj == len(ws) - 1) else None
                            s_op(lambda e, h2=hl, jo=qoff, Wd2=Wd, cl=col, p=pp,
                                 bb=bst: e.activation(
                                phat[p][:, h2 * 256 + jo: h2 * 256 + jo + Wd2],
                                ps[2 * p + h2 // 2][:, (h2 % 2) * 256 + jo:
                                                    (h2 % 2) * 256 + jo + Wd2],
                                AF.Exp, bias=bb[:, cl:cl + 1], scale=ISQ), fin)
                            qoff += Wd
                    v_exp = s_px.n
                    masked = c >= L - 4
                    v_pm = 0
                    if masked:
                        w("vector", s_px, v_exp)
                        v_op(lambda e, cc=c, p=pp, Li=L: e.tensor_tensor(
                            phat[p], phat[p],
                            mask_st[:, (cc - (Li - 4)) * 1024:(cc - (Li - 4) + 1) * 1024],
                            ALU.mult), s_pm)
                        v_pm = s_pm.n
                    if pend is not None:
                        emit_av(pend)
                    pend = (c, pp, acts, masked, v_exp, v_pm)
                    n_chunk += 1
                emit_av(pend)
                ssc_grp.append(s_sc.n)
                spx_grp.append(s_px.n)
                pending_fin[0] = make_finalize(i, g, s_av.n)
        flush_fin()
        ab.pop(); af.pop()
        barrier()

        ab.pop()
        ab.push(); af.push()
        outT_sb = ab.take(MT * QR)
        wot_sb = ab.take(16 * 1024)
        x_st = [af.take(512) for _ in range(2)]
        v_st = [af.take(512) for _ in range(2)]

        w("sync", stB, stB.n)
        dma("sync", ldB, outT_sb.rearrange("p (a q) -> p a q", q=QR),
            outT.rearrange("(a p) q -> p a q", p=128))
        v_ot = ldB.n
        w("tensor", ldB, v_ot)
        it = 0
        te_db = []
        ve_o, st_o = [], []
        for db in range(8):
            wsl = wot_sb
            if db >= 1:
                w("gpsimd", te, te_db[db - 1])
            dma("gpsimd", ldD, wsl.rearrange("p (a d) -> p a d", d=512),
                WoT[:, db * 512:(db + 1) * 512].rearrange("(a p) d -> p a d", p=128))
            v_w = ldD.n
            w("tensor", ldD, v_w)
            for qt in range(QT):
                bk = it % 2
                pA = ps[bk]
                if it >= 2:
                    w("scalar", ve, ve_o[it - 2])
                dma("scalar", ldC, x_st[it % 2],
                    x_q[qt * 128:(qt + 1) * 128, db * 512:(db + 1) * 512])
                if it >= 2:
                    w("tensor", ve, ve_o[it - 2])
                for cc in range(MT):
                    fin = te if cc == MT - 1 else None
                    t_op(lambda e, c2=cc, q2=qt, wl=wsl, f=(cc == 0), a=pA: e.matmul(
                        a[:, 0:512], outT_sb[:, c2 * QR + q2 * 128:c2 * QR + (q2 + 1) * 128],
                        wl[:, c2 * 512:(c2 + 1) * 512],
                        start=f, stop=(c2 == MT - 1)), fin)
                v_mm = te.n
                w("vector", te, v_mm)
                w("vector", ldC, ldC.n)
                if it >= 2:
                    w("vector", stA, st_o[it - 2])
                vv = v_op(lambda e, b2=bk: e.tensor_tensor(
                    v_st[b2], ps[b2][:, 0:512], x_st[b2], ALU.add), ve)
                w("sync", ve, vv)
                dma("sync", stA, v_scr[qt * 128:(qt + 1) * 128, db * 512:(db + 1) * 512],
                    v_st[bk])
                ve_o.append(ve.n)
                st_o.append(stA.n)
                it += 1
            te_db.append(te.n)
        ab.pop(); af.pop()
        barrier()

        ln1_nm = asm.take(16)
        ln1_rs = asm.take(16)
        ab.push(); af.push()
        vta = af.take(4096)
        sq_bf6 = ab.take(4096)
        y1t_b = [ab.take(4096) for _ in range(2)]
        c_ln6 = asm.take(16)
        for qt in range(QT):
            if qt >= 1:
                w("sync", ve, ve.n)
                w("sync", se, se.n)
            dma("sync", ldB, vta, v_scr[qt * 128:(qt + 1) * 128, :])
            w("vector", ldB, ldB.n)
            v_op(lambda e: e.reduce_sum(c_ln6[:, 0:1], vta, axis=AX.X))
            vsync()
            vnm = v_op(lambda e, q2=qt: e.tensor_scalar(
                ln1_nm[:, q2:q2 + 1], c_ln6[:, 0:1], -1.0 / DM, None, ALU.mult), ve)
            w("scalar", ve, vnm)
            vsq = s_op(lambda e, q2=qt: e.activation(
                sq_bf6, vta, AF.Square, bias=ln1_nm[:, q2:q2 + 1], scale=1.0,
                accum_out=c_ln6[:, 1:2]), se)
            w("vector", se, vsq)
            v_op(lambda e: e.tensor_scalar(c_ln6[:, 2:3], c_ln6[:, 1:2],
                                           1.0 / DM, EPS_LN, ALU.mult, ALU.add))
            vsync()
            vrc = v_op(lambda e: e.reciprocal(c_ln6[:, 3:4], c_ln6[:, 2:3]), ve)
            w("scalar", ve, vrc)
            vst = s_op(lambda e, q2=qt: e.activation(ln1_rs[:, q2:q2 + 1],
                                                     c_ln6[:, 3:4], AF.Sqrt), se)
            w("vector", se, vst)
            vsync()
            if qt >= 2:
                w("vector", stA, stA.n)
            vy = v_op(lambda e, q2=qt: e.tensor_scalar(
                y1t_b[q2 % 2], vta, ln1_nm[:, q2:q2 + 1], ln1_rs[:, q2:q2 + 1],
                ALU.add, ALU.mult), ve)
            w("sync", ve, vy)
            dma("sync", stA, y1b[qt * 128:(qt + 1) * 128, :], y1t_b[qt % 2])
        ab.pop(); af.pop()
        barrier()

        ab.push(); af.push()
        y1_pan = [ab.take(QR) for _ in range(2)]
        for cc in range(MT):
            pan = y1_pan[cc % 2]
            if cc >= 2:
                w("sync", stA, stA.n)
            dma_t("sync", ldB, pan, y1b[:, cc * 128:(cc + 1) * 128])
            w("sync", ldB, ldB.n)
            dma("sync", stA, y1T[cc * 128:(cc + 1) * 128, :], pan)
        v_y1T = stA.n

        xg = [ab.take(2 * QR) for _ in range(2)]
        ub = [ab.take(2048) for _ in range(2)]
        db_sb = [ab.take(2048) for _ in range(2)]
        mid = ab.take(8 * 512)
        y2h = [ab.take(2 * QR) for _ in range(2)]

        w("gpsimd", stA, v_y1T)
        w("gpsimd", ldA, v_ldA_ph0)

        def ffn_loads(h):
            bb = h % 2
            dma("gpsimd", gp, ub[bb].rearrange("p (ch m) -> p ch m", ch=2),
                upw[h].rearrange("(ch p) m -> p ch m", p=128))
            dma("gpsimd", gp, db_sb[bb].rearrange("p (mc d) -> p mc d", mc=8),
                downw[h].rearrange("(mc p) d -> p mc d", p=128))
            gp.inc(16)
            prog.add("gpsimd", lambda e, h2=h: e.indirect_dma_start(
                out=xg[h2 % 2][:, 0:QR], out_offset=None, in_=y1T,
                in_offset=IndirectOffsetOnAxis(perm_sb[:, 2 * h2:2 * h2 + 1], 0)
            ).then_inc(gp.h, 16))
            gp.inc(16)
            prog.add("gpsimd", lambda e, h2=h: e.indirect_dma_start(
                out=xg[h2 % 2][:, QR:2 * QR], out_offset=None, in_=y1T,
                in_offset=IndirectOffsetOnAxis(perm_sb[:, 2 * h2 + 1:2 * h2 + 2], 0)
            ).then_inc(gp.h, 16))
            return gp.n

        v_ld = [None] * 17
        v_ld[0] = ffn_loads(0)
        gelA_prev, gelB_prev, evac_prev = 0, 0, 0
        for h in range(16):
            w("tensor", gp, v_ld[h])
            if h < 15:
                w("gpsimd", te, te.n)
                w("gpsimd", gp, gp.n)
                v_ld[h + 1] = ffn_loads(h + 1)
            if h >= 2:
                w("vector", gsc, gsc.n)
            for qb in range(4):
                w("tensor", ve, evac_prev)
                w("tensor", se, gelA_prev)
                for mt in range(4):
                    for ch in range(2):
                        fin = te if (mt == 3 and ch == 1) else None
                        t_op(lambda e, m2=mt, c2=ch, b2=h % 2, q2=qb: e.matmul(
                            ps[m2][:, 0:512],
                            ub[b2][:, c2 * 1024 + m2 * 128:c2 * 1024 + (m2 + 1) * 128],
                            xg[b2][:, c2 * QR + q2 * 512:c2 * QR + (q2 + 1) * 512],
                            start=(c2 == 0), stop=(c2 == 1)), fin)
                v_upA = te.n
                w("tensor", se, gelB_prev)
                for mt in range(4, 8):
                    for ch in range(2):
                        fin = te if (mt == 7 and ch == 1) else None
                        t_op(lambda e, m2=mt, c2=ch, b2=h % 2, q2=qb: e.matmul(
                            ps[m2][:, 0:512],
                            ub[b2][:, c2 * 1024 + m2 * 128:c2 * 1024 + (m2 + 1) * 128],
                            xg[b2][:, c2 * QR + q2 * 512:c2 * QR + (q2 + 1) * 512],
                            start=(c2 == 0), stop=(c2 == 1)), fin)
                v_upB = te.n
                w("scalar", te, v_upA)
                for mt in range(4):
                    fin = se if mt == 3 else None
                    s_op(lambda e, m2=mt: e.activation(
                        mid[:, m2 * 512:(m2 + 1) * 512], ps[m2][:, 0:512],
                        AF.Gelu), fin)
                gelA_prev = se.n
                w("scalar", te, v_upB)
                for mt in range(4, 8):
                    fin = se if mt == 7 else None
                    s_op(lambda e, m2=mt: e.activation(
                        mid[:, m2 * 512:(m2 + 1) * 512], ps[m2][:, 0:512],
                        AF.Gelu), fin)
                gelB_prev = se.n
                w("tensor", se, gelA_prev)
                for dt in range(2):
                    for mc2 in range(4):
                        t_op(lambda e, d2=dt, m2=mc2, b2=h % 2: e.matmul(
                            ps[d2][:, 0:512],
                            db_sb[b2][:, m2 * 256 + d2 * 128:m2 * 256 + (d2 + 1) * 128],
                            mid[:, m2 * 512:(m2 + 1) * 512],
                            start=(m2 == 0), stop=False))
                w("tensor", se, gelB_prev)
                for dt in range(2):
                    for mc2 in range(4, 8):
                        fin = te if (dt == 1 and mc2 == 7) else None
                        t_op(lambda e, d2=dt, m2=mc2, b2=h % 2: e.matmul(
                            ps[d2][:, 0:512],
                            db_sb[b2][:, m2 * 256 + d2 * 128:m2 * 256 + (d2 + 1) * 128],
                            mid[:, m2 * 512:(m2 + 1) * 512],
                            start=False, stop=(m2 == 7)), fin)
                v_dn = te.n
                w("vector", te, v_dn)
                for dt in range(2):
                    fin = ve if dt == 1 else None
                    v_op(lambda e, d2=dt, b2=h % 2, q2=qb: e.tensor_copy(
                        y2h[b2][:, d2 * QR + q2 * 512:d2 * QR + (q2 + 1) * 512],
                        ps[d2][:, 0:512]), fin)
                evac_prev = ve.n
            v_ev = ve.n
            w("gpsimd", ve, v_ev)
            for t2 in range(2):
                gsc.inc(16)
                prog.add("gpsimd", lambda e, h2=h, t3=t2: e.indirect_dma_start(
                    out=y2T, out_offset=IndirectOffsetOnAxis(
                        perm_sb[:, 2 * h2 + t3:2 * h2 + t3 + 1], 0),
                    in_=y2h[h2 % 2][:, t3 * QR:(t3 + 1) * QR], in_offset=None
                ).then_inc(gsc.h, 16))
        ab.pop(); af.pop()
        barrier()

        ab.push(); af.push()
        vt_f = af.take(4096)
        y2_b = ab.take(4096)
        sq_bf8 = ab.take(4096)
        c_ln8 = asm.take(8)
        w("sync", gsc, gsc.n)
        for qt in range(QT):
            if qt >= 1:
                w("sync", stB, stB.n)
                w("sync", se, se.n)
            dma_t("sync", ldB, y2_b, y2T[:, qt * 128:(qt + 1) * 128])
            dma("sync", ldB, vt_f, v_scr[qt * 128:(qt + 1) * 128, :])
            w("vector", ldB, ldB.n)
            v_op(lambda e, q2=qt: e.tensor_scalar(
                vt_f, vt_f, ln1_nm[:, q2:q2 + 1], ln1_rs[:, q2:q2 + 1],
                ALU.add, ALU.mult))
            vsync()
            v_op(lambda e: e.tensor_tensor(vt_f, vt_f, y2_b, ALU.add))
            vsync()
            v_op(lambda e: e.reduce_sum(c_ln8[:, 0:1], vt_f, axis=AX.X))
            vsync()
            vnm = v_op(lambda e: e.tensor_scalar(c_ln8[:, 1:2], c_ln8[:, 0:1],
                                                 -1.0 / DM, None, ALU.mult), ve)
            w("scalar", ve, vnm)
            vsq = s_op(lambda e: e.activation(sq_bf8, vt_f, AF.Square,
                                              bias=c_ln8[:, 1:2], scale=1.0,
                                              accum_out=c_ln8[:, 2:3]), se)
            w("vector", se, vsq)
            v_op(lambda e: e.tensor_scalar(c_ln8[:, 3:4], c_ln8[:, 2:3],
                                           1.0 / DM, EPS_LN, ALU.mult, ALU.add))
            vsync()
            vrc = v_op(lambda e: e.reciprocal(c_ln8[:, 4:5], c_ln8[:, 3:4]), ve)
            w("scalar", ve, vrc)
            vst = s_op(lambda e: e.activation(c_ln8[:, 5:6], c_ln8[:, 4:5], AF.Sqrt), se)
            w("vector", se, vst)
            vsync()
            vy = v_op(lambda e: e.tensor_scalar(
                vt_f, vt_f, c_ln8[:, 1:2], c_ln8[:, 5:6], ALU.add, ALU.mult), ve)
            w("sync", ve, vy)
            dma("sync", stB, out[qt * 128:(qt + 1) * 128, :], vt_f)
        ab.pop()

        for e in ENGS:
            for c in (stA, stB, gp, ldA, ldB, ldC, ldD):
                if c.n:
                    prog.add(e, lambda eng, s=c.h, v=c.n: eng.wait_ge(s, v))

        def run_eng(name):
            def body(eng):
                for fn in prog.items[name]:
                    fn(eng)
            return body

        block.sync(run_eng("sync"))
        block.tensor(run_eng("tensor"))
        block.vector(run_eng("vector"))
        block.scalar(run_eng("scalar"))
        block.gpsimd(run_eng("gpsimd"))

    return nc



def make_inputs(core, full):
    b, half = core // 2, core % 2
    ranges = core_ranges(core)
    x = np.asarray(full["L0_attn_x"])
    x_q = np.concatenate([x[b, 256 * r:256 * r + 256, :] for r in ranges], axis=0)
    bias, mask = build_tables(core)
    cache = full.setdefault("_t_cache", {})
    def tr(key, arr):
        if key not in cache:
            cache[key] = np.ascontiguousarray(np.asarray(arr).T)
        return cache[key]
    return {
        "x_q": np.ascontiguousarray(x_q),
        "x_qT": np.ascontiguousarray(x_q.T),
        "x_kvT": tr(("xkv", b), x[b]),
        "WqT": tr("wq", full["L0_attn_Wq"]),
        "WkT": tr("wk", full["L0_attn_Wk"]),
        "WvT": tr("wv", full["L0_attn_Wv"]),
        "WoT": tr("wo", full["L0_attn_Wo"]),
        "qw": np.asarray(full["L0_attn_q_norm_w"]).reshape(1, HD),
        "kw": np.asarray(full["L0_attn_k_norm_w"]).reshape(1, HD),
        "upw": np.asarray(full["L0_ffn_up"]),
        "downw": np.asarray(full["L0_ffn_down"]),
        "perm": np.asarray(full["L0_ffn_perm"]).astype(np.int32),
        "bias_tab": bias,
        "mask_tab": mask,
        "ones_row": np.ones((1, 128), dtype=np.float32),
    }


def kernel(**inputs):
    nc = build_program()
    in_maps = [make_inputs(c, inputs) for c in range(NCORE)]
    res = run_bass_kernel_spmd(nc, in_maps, core_ids=list(range(NCORE)))
    y = np.empty((B, S, DM), dtype=np.float32)
    for c in range(NCORE):
        o = res.results[c]["out"]
        for i, r in enumerate(core_ranges(c)):
            y[c // 2, 256 * r:256 * r + 256, :] = o[256 * i:256 * (i + 1), :]
    return y
